# version 02
# baseline (speedup 1.0000x reference)
import numpy as np
import ml_dtypes
import concourse.bass as bass
import concourse.mybir as mybir
from concourse.bass_utils import run_bass_kernel_spmd
from concourse.bass_core_rust import IndirectOffsetOnAxis
from contextlib import ExitStack

F32 = mybir.dt.float32
BF16 = mybir.dt.bfloat16
I32 = mybir.dt.int32
AF = mybir.ActivationFunctionType
ALU = mybir.AluOpType
AX = mybir.AxisListType
BF = ml_dtypes.bfloat16

P = 128
D = 1024
HC = 8
DI = 2048
G = 16
S = 16
RK = 64
TC = 512
CK = 36
NB = 8
MC = 128
NSUB = TC // MC
ALPHA = 0.1
EPS = 1e-5
FF = 4096
FC = 32
F2 = 2048
F2C = 16
NE = 8
TOK_OWN = 4096
HALF = 2048

ENGS = ("sync", "gpsimd", "scalar", "tensor", "vector")
SEMS = ("sQ", "sR", "sG", "sP", "sA", "sV", "sL")
C_M1, C_EPS, C_MH, C_ALPHA, C_RD = 0, 1, 2, 3, 4
CONSTS = np.zeros((P, 8), np.float32)
CONSTS[:, C_M1] = -1.0
CONSTS[:, C_EPS] = EPS
CONSTS[:, C_MH] = -0.5
CONSTS[:, C_ALPHA] = ALPHA
CONSTS[:, C_RD] = 1.0 / D

DBG_PREP = [None]
DEBUG_RHS_BAR = [False]
EMIT_DBG = [None]
STAGE = 3


class _MockInstr:
    def then_inc(self, *a, **k):
        return self


class _MockEngine:
    def __getattr__(self, name):
        return lambda *a, **k: _MockInstr()


class Bld:

    def __init__(self, nc, stk):
        self.nc = nc
        self.h = {}
        self.c = {}
        for name in SEMS:
            self.h[name] = stk.enter_context(nc.semaphore(name))
            self.c[name] = 0
        self.pending = []
        self._cur = None
        self.mode = "plan"

    def inc(self, instr, sem, n=1):
        if self.mode == "plan":
            self.c[sem] += n
            self._cur["incs"].append((sem, n))
            return self.c[sem]
        else:
            s_, n_ = self._cur["incs"][self._cur["ii"]]
            self._cur["ii"] += 1
            instr.then_inc(self.h[s_], n_)
            return None

    def incd(self, instr, sem):
        return self.inc(instr, sem, 16)

    def wait(self, e, *sems):
        if self.mode == "plan":
            rec = [(s_, self.c[s_]) for s_ in sems if self.c[s_] > 0]
            self._cur["waits"].append(rec)
        else:
            rec = self._cur["waits"][self._cur["wi"]]
            self._cur["wi"] += 1
            for s_, v in rec:
                e.wait_ge(self.h[s_], v)

    def wait_at(self, e, sem, val):
        if self.mode == "plan":
            self._cur["waits"].append([(sem, val)] if val > 0 else [])
        else:
            rec = self._cur["waits"][self._cur["wi"]]
            self._cur["wi"] += 1
            for s_, v in rec:
                e.wait_ge(self.h[s_], v)

    def step(self, eng, f):
        rec = {"eng": eng, "f": f, "waits": [], "incs": []}
        self._cur = rec
        f(_MockEngine())
        self._cur = None
        self.pending.append(rec)

    def flush(self):
        if not self.pending:
            return
        pending = self.pending
        self.pending = []
        self.mode = "emit"
        with self.nc.Block() as blk:
            for eng in ENGS:
                steps = [r for r in pending if r["eng"] == eng]
                if not steps:
                    continue

                def _run(e, steps=steps):
                    for r in steps:
                        self._cur = r
                        r["wi"] = 0
                        r["ii"] = 0
                        r["f"](e)
                    self._cur = None
                getattr(blk, eng)(_run)
        self.mode = "plan"

    def bar(self):
        self.flush()
        snap = dict(self.c)
        with self.nc.Block() as blk:
            for eng in ENGS:
                def _f(e, snap=snap):
                    for s_ in SEMS:
                        if snap[s_] > 0:
                            e.wait_ge(self.h[s_], snap[s_])
                getattr(blk, eng)(_f)


def bcast_hc(ap, hc_n, tc):
    return ap.unsqueeze(1).broadcast_to([P, hc_n, tc])


def build_nc(debug=False):
    nc = bass.Bass("TRN2", target_bir_lowering=False)

    x_cat = nc.dram_tensor("x_cat", [D, CK * TC], F32, kind="ExternalInput").ap()
    flags = nc.dram_tensor("flags", [P, CK], F32, kind="ExternalInput").ap()
    gidx = nc.dram_tensor("gidx", [P, 32], I32, kind="ExternalInput").ap()

    win = {}
    for name, shape in [
        ("inT_h", [32, P, HC, P]), ("outT_h", [G, P, HC, P]),
        ("wrT_h", [HC, P, HC, P]), ("rdT_h", [HC, P, HC, P]),
        ("dtT_h", [RK, DI]), ("xpT_h", [P, G, 96]), ("rw_h", [P, HC, NE]),
        ("up_h", [NE, FC, P, HC, P]), ("down_h", [NE, HC, P, FC, P]),
        ("sp0_h", [2, FC, P, FC, P]), ("sp1a_h", [2, F2C, P, FC, P]),
        ("sp1b_h", [2, FC, P, F2C, P]), ("sp2_h", [2, FC, P, FC, P]),
        ("A_h", [P, G, S]), ("cw_h", [P, G, 4]),
        ("cb_h", [P, G]), ("dtb_h", [P, G]), ("dss_h", [P, G]),
        ("inb_h", [P, 32]), ("outb_h", [P, HC]),
        ("n1_h", [P, HC]), ("n2_h", [P, HC]),
        ("lng_h", [P, HC]), ("lnb_h", [P, HC]),
        ("upb_h", [P, NE, FC]), ("downb_h", [P, NE, HC]),
        ("sp0b_h", [P, 2, FC]), ("ln0g_h", [P, 2, FC]), ("ln0b_h", [P, 2, FC]),
        ("sp1ab_h", [P, 2, F2C]), ("sp1bb_h", [P, 2, FC]),
        ("ln1g_h", [P, 2, FC]), ("ln1b_h", [P, 2, FC]),
        ("sp2b_h", [P, 2, FC]),
        ("decay", [1, P]),
    ]:
        win[name] = nc.dram_tensor(name, shape, F32, kind="ExternalInput").ap()

    out_own = nc.dram_tensor("out_own", [TOK_OWN, D], F32, kind="ExternalOutput").ap()

    sc = {}
    sc["inT"] = nc.dram_tensor("inT_sc", [32, P, HC, P], BF16).ap()
    sc["outT"] = nc.dram_tensor("outT_sc", [G, P, HC, P], BF16).ap()
    sc["wrT"] = nc.dram_tensor("wrT_sc", [HC, P, HC, P], BF16).ap()
    sc["rdT"] = nc.dram_tensor("rdT_sc", [HC, P, HC, P], BF16).ap()
    sc["up"] = nc.dram_tensor("up_sc", [NE, FC, P, HC, P], BF16).ap()
    sc["down"] = nc.dram_tensor("down_sc", [NE, HC, P, FC, P], BF16).ap()
    sc["sp0"] = nc.dram_tensor("sp0_sc", [2, FC, P, FC, P], BF16).ap()
    sc["sp1a"] = nc.dram_tensor("sp1a_sc", [2, F2C, P, FC, P], BF16).ap()
    sc["sp1b"] = nc.dram_tensor("sp1b_sc", [2, FC, P, F2C, P], BF16).ap()
    sc["sp2"] = nc.dram_tensor("sp2_sc", [2, FC, P, FC, P], BF16).ap()
    sc["bc"] = nc.dram_tensor("bc_d", [2, 2 * S, TC], BF16).ap()
    sc["row"] = nc.dram_tensor("row_d", [8, 4096], F32).ap()
    sc["xA"] = nc.dram_tensor("xA_cat", [CK * TC, D], BF16).ap()
    sc["xo"] = nc.dram_tensor("xo_cat", [CK * TC, D], BF16).ap()
    sc["nx"] = nc.dram_tensor("nx_cat", [CK * TC, D], BF16).ap()
    sc["h1"] = nc.dram_tensor("h1_sp", [FC, P, HALF], BF16).ap()
    sc["h2"] = nc.dram_tensor("h2_sp", [FC, P, HALF], BF16).ap()
    sc["t1"] = nc.dram_tensor("t1_sp", [F2C, P, HALF], BF16).ap()
    sc["spp"] = nc.dram_tensor("spp_sp", [FC, P, HALF], BF16).ap()
    sc["wrt"] = nc.dram_tensor("wrt_d", [NE, HALF], BF16).ap()

    dbg = {}
    if debug:
        dbg["x1"] = nc.dram_tensor("dbg_x1", [D, CK * TC], F32, kind="ExternalOutput").ap()
        if STAGE >= 2:
            dbg["ao"] = nc.dram_tensor("dbg_ao", [D, CK * TC], F32, kind="ExternalOutput").ap()
            dbg["nx"] = nc.dram_tensor("dbg_nx", [CK * TC, D], BF16, kind="ExternalOutput").ap()
            dbg["o2"] = nc.dram_tensor("dbg_o2", [D, CK * TC], F32, kind="ExternalOutput").ap()
            dbg["mt"] = nc.dram_tensor("dbg_mt", [P, MC], F32, kind="ExternalOutput").ap()
            dbg["gam"] = nc.dram_tensor("dbg_gam", [P, 8], F32, kind="ExternalOutput").ap()
            dbg["gin"] = nc.dram_tensor("dbg_gin", [P, MC], F32, kind="ExternalOutput").ap()
            dbg["gid"] = nc.dram_tensor("dbg_gid", [P, P], F32, kind="ExternalOutput").ap()
            if STAGE >= 3:
                dbg["h1"] = nc.dram_tensor("dbg_h1", [FC * P, HALF], BF16, kind="ExternalOutput").ap()
                dbg["h2"] = nc.dram_tensor("dbg_h2", [FC * P, HALF], BF16, kind="ExternalOutput").ap()
                dbg["wrt"] = nc.dram_tensor("dbg_wrt", [NE, HALF], F32, kind="ExternalOutput").ap()
                dbg["acc"] = nc.dram_tensor("dbg_acc", [P, HC * HALF], F32, kind="ExternalOutput").ap()
            dbg["rdd"] = nc.dram_tensor("dbg_rdd", [D, CK * TC], F32, kind="ExternalOutput").ap()

    il = {}
    il["id_bf"] = nc.inline_tensor(np.eye(P, dtype=BF), "il_id_bf").ap()
    il["id_f"] = nc.inline_tensor(np.eye(P, dtype=np.float32), "il_id_f").ap()
    il["ones_col"] = nc.inline_tensor(np.ones((P, 1), dtype=BF), "il_ones_col").ap()
    il["ones_row"] = nc.inline_tensor(np.ones((1, P), dtype=BF), "il_ones_row").ap()
    mt_e = np.maximum(np.arange(MC)[None, :] - 1 - np.arange(MC)[:, None], 0).astype(np.float32)
    mt_c = (np.arange(MC)[None, :] > np.arange(MC)[:, None]).astype(np.float32)
    il["mt_exp"] = nc.inline_tensor(mt_e, "il_mt_exp").ap()
    il["mt_cau"] = nc.inline_tensor(mt_c, "il_mt_cau").ap()
    il["iot_c"] = nc.inline_tensor(np.arange(MC, dtype=np.float32).reshape(1, MC), "il_iot_c").ap()
    il["iot_w"] = nc.inline_tensor((MC - 1 - np.arange(MC)).astype(np.float32).reshape(1, MC), "il_iot_w").ap()
    il["consts"] = nc.inline_tensor(CONSTS, "il_consts").ap()

    with ExitStack() as stk:
        bld = Bld(nc, stk)

        def sb(name, shape, dtype):
            return stk.enter_context(nc.sbuf_tensor(name, shape, dtype))

        T = {}
        T["A"] = sb("tA", [P, G * S], F32)
        T["cw"] = sb("tcw", [P, G * 4], F32)
        T["cb"] = sb("tcb", [P, G], F32)
        T["dtb"] = sb("tdtb", [P, G], F32)
        T["dss"] = sb("tdss", [P, G], F32)
        T["inb"] = sb("tinb", [P, 32], F32)
        T["outb"] = sb("toutb", [P, HC], F32)
        T["n1"] = sb("tn1", [P, HC], F32)
        T["n2"] = sb("tn2", [P, HC], F32)
        T["lng"] = sb("tlng", [P, HC], F32)
        T["lnb"] = sb("tlnb", [P, HC], F32)
        T["flag"] = sb("tflag", [P, CK], F32)
        T["idb"] = sb("tidb", [P, P], BF16)
        T["ones"] = sb("tones", [P, 1], BF16)
        T["onesr"] = sb("tonesr", [1, P], BF16)
        T["cst"] = sb("tcst", [P, 8], F32)
        T["dtT"] = sb("tdtT", [RK, G * P], BF16)
        T["xpT"] = sb("txpT", [P, G * 96], BF16)
        T["rwT"] = sb("trwT", [P, HC * NE], BF16)
        T["gam"] = sb("tgam", [P, MC], F32)
        T["gin"] = sb("tgin", [P, MC], BF16)
        T["gw"] = sb("tgw", [P, 1], F32)
        T["gmc"] = sb("tgmc", [P, 1], F32)
        T["MT"] = sb("tMT", [P, MC], F32)
        T["gId"] = sb("tgId", [P, P], BF16)
        T["stat"] = sb("tstat", [1, 5 * TC], F32)
        T["arow"] = sb("tarow", [1, TC], BF16)
        T["brow"] = sb("tbrow", [1, TC], BF16)
        T["pA"] = stk.enter_context(nc.psum_tensor("pA", [P, 512], F32))
        T["pB"] = stk.enter_context(nc.psum_tensor("pB", [P, 512], F32))
        T["pC"] = stk.enter_context(nc.psum_tensor("pC", [P, 512], F32))
        T["pD"] = stk.enter_context(nc.psum_tensor("pD", [P, 512], F32))
        T["pE"] = stk.enter_context(nc.psum_tensor("pE", [P, 512], F32))
        T["pF"] = stk.enter_context(nc.psum_tensor("pF", [P, 512], F32))
        T["pG"] = stk.enter_context(nc.psum_tensor("pG", [P, 512], F32))
        T["pT"] = stk.enter_context(nc.psum_tensor("pT", [P, 1024], BF16))

        DBG_PREP[0] = dbg if (debug and "mt" in dbg) else None
        EMIT_DBG[0] = dbg if (debug and "acc" in dbg) else None
        emit_prep(nc, bld, win, sc, il, T, flags)

        with ExitStack() as astk:
            def asb(name, shape, dtype):
                return astk.enter_context(nc.sbuf_tensor(name, shape, dtype))

            A = {}
            A["wbufA"] = asb("wbufA", [P, 2 * 1024], BF16)
            A["wbufB"] = asb("wbufB", [P, 2 * 1024], BF16)
            A["xbf"] = asb("xbf", [P, HC * TC], BF16)
            A["stA"] = asb("stA", [P, HC * TC], BF16)
            A["stB"] = asb("stB", [P, HC * TC], BF16)
            A["stC"] = asb("stC", [P, HC * TC], BF16)
            A["xmp"] = asb("xmp", [P, G * (TC + 3)], BF16)
            A["xs"] = asb("xs", [P, G * TC], BF16)
            A["res"] = asb("res", [P, G * TC], BF16)
            A["U1"] = asb("U1", [P, 2 * G * TC], BF16)
            A["hg"] = asb("hg", [P, S * TC], BF16)
            A["U2"] = asb("U2", [P, S * TC], BF16)
            A["xdbl"] = asb("xdbl", [96, TC], BF16)
            A["delta"] = asb("delta", [P, TC], F32)
            A["dsc"] = asb("dsc", [P, TC], F32)
            A["yg"] = asb("yg", [P, TC], BF16)
            A["st1"] = asb("st1s", [P, G * S], F32)
            A["st2"] = asb("st2s", [P, G * S], F32)
            A["cr1"] = asb("cr1", [P, G * 3], BF16)
            A["cr2"] = asb("cr2", [P, G * 3], BF16)
            A["o2c"] = asb("o2c", [P, HC], BF16)
            A["Wt"] = asb("Wt", [P, NE * D], BF16)
            A["o2e"] = asb("o2e", [P, HC * (TC + 1)], BF16)
            A["cacc"] = A["U1"][:].bitcast(F32)
            A["dA"] = A["U1"][:, 0:S * TC]
            A["w"] = A["U1"][:, S * TC:2 * S * TC]
            A["vfm"] = A["U1"][:, 0:HC * TC]
            A["reads"] = A["U1"][:, HC * TC:2 * HC * TC]
            A["Bv"] = A["U2"][:, 0:S * TC]
            A["sq"] = A["U2"][:, 0:HC * TC]
            ub = A["U1"][:]
            A["vtm"] = ub[:, 8192:8192 + D]
            A["vtm2"] = ub[:, 9216:9216 + D]
            A["wktm"] = ub[:, 10240:10240 + D]
            A["rksc"] = ub[:, 11264:11264 + HC * MC]
            A["SM"] = ub[:, 12288:12288 + MC]
            A["tmt"] = ub[:, 12416:12416 + D]

            emit_attn(nc, bld, win, sc, il, T, A, x_cat, dbg)

        bld.bar()

        if STAGE >= 3:
            with ExitStack() as fstk:
                def fsb(name, shape, dtype):
                    return fstk.enter_context(nc.sbuf_tensor(name, shape, dtype))
                Ff = {}
                Ff["nxF"] = fsb("nxF", [P, HC * HALF], BF16)
                Ff["acc"] = fsb("accF", [P, HC * HALF], F32)
                Ff["rhs"] = fsb("rhsF", [P, FC * 512], BF16)
                Ff["wst"] = fsb("wstF", [P, 2 * 4096], BF16)
                Ff["ev"] = fsb("evF", [P, 512], BF16)
                Ff["ev2"] = fsb("ev2F", [P, 512], BF16)
                Ff["ev3"] = fsb("ev3F", [P, 512], BF16)
                Ff["ev4"] = fsb("ev4F", [P, 512], BF16)
                Ff["evf"] = fsb("evfF", [P, 512], F32)
                Ff["m1"] = fsb("m1F", [P, 16], F32)
                Ff["s1"] = fsb("s1F", [P, 16], F32)
                Ff["s2"] = fsb("s2F", [P, 16], F32)
                Ff["w2"] = fsb("w2F", [P, 16 * NE], F32)
                Ff["outs"] = fsb("outsF", [P, D], F32)
                Ff["sq"] = fsb("sqF", [P, 512], BF16)
                Ff["gat"] = fsb("gatF", [P, D], BF16)
                Ff["wbc"] = fsb("wbcF", [P, HALF], BF16)
                Ff["lg"] = fsb("lgF", [8, HALF], F32)
                Ff["ptm"] = fsb("ptmF", [P, 16 * NE], F32)
                Ff["msk"] = fsb("mskF", [P, 16 * NE], F32)
                Ff["w8"] = fsb("w8F", [8, HALF], BF16)
                Ff["idx"] = fsb("idxF", [P, 32], I32)
                Ff["idf"] = fsb("tidf", [P, P], F32)
                Ff["upb"] = fsb("tupb", [P, NE * FC], F32)
                Ff["dwb"] = fsb("tdwb", [P, NE * HC], F32)
                Ff["s0b"] = fsb("ts0b", [P, 2 * FC], F32)
                Ff["l0g"] = fsb("tl0g", [P, 2 * FC], F32)
                Ff["l0b"] = fsb("tl0b", [P, 2 * FC], F32)
                Ff["s1ab"] = fsb("ts1ab", [P, 2 * F2C], F32)
                Ff["s1bb"] = fsb("ts1bb", [P, 2 * FC], F32)
                Ff["l1g"] = fsb("tl1g", [P, 2 * FC], F32)
                Ff["l1b"] = fsb("tl1b", [P, 2 * FC], F32)
                Ff["s2b"] = fsb("ts2b", [P, 2 * FC], F32)
                emit_ffn(nc, bld, win, sc, il, T, Ff, gidx, out_own, dbg)

        bld.bar()

    return nc


def ln_stats(nc, bld, T, src_ap, hc_n, sq_ap, with_mean, tc=TC):
    pC, pD, pE = T["pC"], T["pD"], T["pE"]
    stat = T["stat"]

    def _sq(e):
        bld.wait(e, "sV", "sP", "sQ", "sG")
        inst = e.activation(sq_ap, src_ap, AF.Square)
        bld.inc(inst, "sA")
    bld.step("scalar", _sq)

    def _mm(e):
        bld.wait(e, "sA", "sV")
        inst = None
        for hc in range(hc_n):
            inst = e.matmul(pC[0:1, 0:tc], T["ones"][:],
                            sq_ap[:, hc * tc:(hc + 1) * tc],
                            start=(hc == 0), stop=(hc == hc_n - 1))
        if with_mean:
            for hc in range(hc_n):
                inst = e.matmul(pD[0:1, 0:tc], T["ones"][:],
                                src_ap[:, hc * tc:(hc + 1) * tc],
                                start=(hc == 0), stop=(hc == hc_n - 1))
        bld.inc(inst, "sP")
    bld.step("tensor", _mm)

    def _rows(e):
        bld.wait(e, "sP")
        e.tensor_scalar_mul(stat[0:1, 1 * TC:1 * TC + tc], pC[0:1, 0:tc], 1.0 / D)
        if with_mean:
            e.tensor_scalar_mul(stat[0:1, 0 * TC:0 * TC + tc], pD[0:1, 0:tc], 1.0 / D)
            e.tensor_mul(stat[0:1, 2 * TC:2 * TC + tc], stat[0:1, 0 * TC:0 * TC + tc], stat[0:1, 0 * TC:0 * TC + tc])
            inst = e.tensor_sub(stat[0:1, 2 * TC:2 * TC + tc], stat[0:1, 1 * TC:1 * TC + tc], stat[0:1, 2 * TC:2 * TC + tc])
        else:
            inst = e.tensor_copy(stat[0:1, 2 * TC:2 * TC + tc], stat[0:1, 1 * TC:1 * TC + tc])
        bld.inc(inst, "sV")
    bld.step("vector", _rows)

    def _rstd(e):
        bld.wait(e, "sV")
        e.activation(stat[0:1, 3 * TC:3 * TC + tc], stat[0:1, 2 * TC:2 * TC + tc], AF.Ln, bias=T["cst"][0:1, C_EPS:C_EPS + 1])
        e.activation(stat[0:1, 3 * TC:3 * TC + tc], stat[0:1, 3 * TC:3 * TC + tc], AF.Exp, scale=T["cst"][0:1, C_MH:C_MH + 1])
        inst = e.activation(T["arow"][0:1, 0:tc], stat[0:1, 3 * TC:3 * TC + tc], AF.Copy)
        bld.inc(inst, "sA")
    bld.step("scalar", _rstd)

    if with_mean:
        def _b(e):
            bld.wait(e, "sA")
            e.tensor_mul(stat[0:1, 4 * TC:4 * TC + tc], stat[0:1, 0 * TC:0 * TC + tc], stat[0:1, 3 * TC:3 * TC + tc])
            inst = e.tensor_scalar_mul(T["brow"][0:1, 0:tc], stat[0:1, 4 * TC:4 * TC + tc], -1.0)
            bld.inc(inst, "sV")
        bld.step("vector", _b)

    def _bc(e):
        bld.wait(e, "sA", "sV")
        inst = e.matmul(pD[:, 0:tc], T["onesr"][:], T["arow"][0:1, 0:tc], start=True, stop=True)
        if with_mean:
            inst = e.matmul(pE[:, 0:tc], T["onesr"][:], T["brow"][0:1, 0:tc], start=True, stop=True)
        bld.inc(inst, "sP")
    bld.step("tensor", _bc)


def ln_apply(nc, bld, T, dst_ap, src_ap, hc_n, with_mean, tc=TC, gains=None, engine="vector"):
    pD, pE = T["pD"], T["pE"]

    def _ap(e):
        bld.wait(e, "sP")
        inst = None
        for hc in range(hc_n):
            sl = (hc * tc, (hc + 1) * tc)
            inst = e.tensor_mul(dst_ap[:, sl[0]:sl[1]], src_ap[:, sl[0]:sl[1]], pD[:, 0:tc])
            if with_mean:
                inst = e.tensor_add(dst_ap[:, sl[0]:sl[1]], dst_ap[:, sl[0]:sl[1]], pE[:, 0:tc])
        bld.inc(inst, "sV")
    bld.step("vector", _ap)

    if gains is not None:
        gt, bt = gains

        def _g(e):
            bld.wait(e, "sV")
            inst = None
            for hc in range(hc_n):
                sl = (hc * tc, (hc + 1) * tc)
                kw = dict(scale=gt[:, hc:hc + 1])
                if bt is not None:
                    kw["bias"] = bt[:, hc:hc + 1]
                inst = e.activation(dst_ap[:, sl[0]:sl[1]], dst_ap[:, sl[0]:sl[1]], AF.Identity, **kw)
            bld.inc(inst, "sA")
        bld.step("scalar", _g)


def emit_prep(nc, bld, win, sc, il, T, flags):
    def _small(e):
        ld = lambda d, src: bld.incd(e.dma_start(out=d, in_=src), "sQ")
        ld(T["A"][:].rearrange("p (g s) -> p g s", g=G), win["A_h"])
        ld(T["cw"][:].rearrange("p (g k) -> p g k", g=G), win["cw_h"])
        ld(T["cb"][:], win["cb_h"])
        ld(T["dtb"][:], win["dtb_h"])
        ld(T["dss"][:], win["dss_h"])
        ld(T["inb"][:], win["inb_h"])
        ld(T["outb"][:], win["outb_h"])
        ld(T["n1"][:], win["n1_h"])
        ld(T["n2"][:], win["n2_h"])
        ld(T["lng"][:], win["lng_h"])
        ld(T["lnb"][:], win["lnb_h"])
        ld(T["flag"][:], flags)
        ld(T["MT"][:], il["mt_exp"])
        ld(T["cst"][:], il["consts"])
        ld(T["gam"][0:1, 0:MC], win["decay"])
        ld(T["stat"][0:1, 0 * TC:0 * TC + MC], il["iot_c"])
        ld(T["stat"][0:1, 1 * TC:1 * TC + MC], il["iot_w"])
    bld.step("sync", _small)

    def _small2(e):
        ld = lambda d, src: bld.incd(e.dma_start(out=d, in_=src), "sG")
        ld(T["idb"][:], il["id_bf"])
        ld(T["ones"][:], il["ones_col"])
        ld(T["onesr"][:], il["ones_row"])
        ld(T["rwT"][:].rearrange("p (hc e) -> p hc e", hc=HC), win["rw_h"])
        ld(T["dtT"][:], win["dtT_h"])
        ld(T["xpT"][:].rearrange("p (g o) -> p g o", g=G), win["xpT_h"])
    bld.step("gpsimd", _small2)

    def flat(ap):
        n = 1
        for d in ap.shape:
            n *= d
        dims = " ".join(f"d{i}" for i in range(len(ap.shape)))
        grouped = "(" + dims + ")"
        rows = n // (P * P)
        return ap.rearrange(f"{dims} -> {grouped}").unsqueeze(0).rearrange(
            "o (r c) -> (o r) c", c=P * P)

    def _d2d(e):
        pairs = [
            (sc["inT"], win["inT_h"]), (sc["outT"], win["outT_h"]),
            (sc["wrT"], win["wrT_h"]), (sc["rdT"], win["rdT_h"]),
        ]
        if STAGE >= 3:
            for ee in range(NE):
                pairs.append((sc["up"][ee], win["up_h"][ee]))
                pairs.append((sc["down"][ee], win["down_h"][ee]))
            for j2 in range(2):
                pairs.append((sc["sp0"][j2], win["sp0_h"][j2]))
                pairs.append((sc["sp1a"][j2], win["sp1a_h"][j2]))
                pairs.append((sc["sp1b"][j2], win["sp1b_h"][j2]))
                pairs.append((sc["sp2"][j2], win["sp2_h"][j2]))
        for dst, src in pairs:
            bld.incd(e.dma_start(out=flat(dst), in_=flat(src)), "sG")
    bld.step("gpsimd", _d2d)

    def _g1(e):
        bld.wait(e, "sQ")
        gam, stat = T["gam"], T["stat"]
        st4 = stat[0:1, 4 * TC:4 * TC + MC]
        e.activation(st4, gam[0:1, 0:MC], AF.Abs)
        e.activation(st4, st4, AF.Exp, scale=T["cst"][0:1, C_M1:C_M1 + 1])
        inst = e.activation(st4, st4, AF.Ln, bias=1.0)
        bld.inc(inst, "sA")
    bld.step("scalar", _g1)
    bld.flush()

    def _g1b(e):
        bld.wait(e, "sA")
        gam, stat = T["gam"], T["stat"]
        st3 = stat[0:1, 3 * TC:3 * TC + MC]
        inst = e.activation(st3, gam[0:1, 0:MC], AF.Relu, scale=T["cst"][0:1, C_M1:C_M1 + 1])
        bld.inc(inst, "sA")
    bld.step("scalar", _g1b)
    bld.flush()

    def _g2(e):
        bld.wait(e, "sA")
        gam, stat = T["gam"], T["stat"]
        st3 = stat[0:1, 3 * TC:3 * TC + MC]
        st4 = stat[0:1, 4 * TC:4 * TC + MC]
        e.tensor_add(gam[0:1, 0:MC], st3, st4)
        e.tensor_scalar_mul(gam[0:1, 0:MC], gam[0:1, 0:MC], -1.0)
        e.tensor_mul(stat[0:1, 2 * TC:2 * TC + MC], stat[0:1, 0 * TC:0 * TC + MC], gam[0:1, 0:MC])
        e.tensor_mul(stat[0:1, 3 * TC:3 * TC + MC], stat[0:1, 1 * TC:1 * TC + MC], gam[0:1, 0:MC])
        inst = e.tensor_scalar_mul(stat[0:1, 4 * TC:4 * TC + MC], gam[0:1, 0:MC], float(MC))
        bld.inc(inst, "sV")
    bld.step("vector", _g2)
    bld.flush()

    def _g3(e):
        bld.wait(e, "sV")
        stat = T["stat"]
        e.activation(stat[0:1, 2 * TC:2 * TC + MC], stat[0:1, 2 * TC:2 * TC + MC], AF.Exp)
        e.activation(stat[0:1, 3 * TC:3 * TC + MC], stat[0:1, 3 * TC:3 * TC + MC], AF.Exp)
        inst = e.activation(stat[0:1, 4 * TC:4 * TC + MC], stat[0:1, 4 * TC:4 * TC + MC], AF.Exp)
        bld.inc(inst, "sA")
    bld.step("scalar", _g3)
    bld.flush()

    def _g4(e):
        bld.wait(e, "sA", "sV")
        bld.incd(e.dma_start(out=sc["row"][1:2, 0:MC], in_=T["stat"][0:1, 2 * TC:2 * TC + MC]), "sR")
        bld.incd(e.dma_start(out=sc["row"][2:3, 0:MC], in_=T["stat"][0:1, 3 * TC:3 * TC + MC]), "sR")
        bld.incd(e.dma_start(out=sc["row"][3:4, 0:MC], in_=T["stat"][0:1, 4 * TC:4 * TC + MC]), "sR")
        bld.incd(e.dma_start(out=sc["row"][4:5, 0:MC], in_=T["gam"][0:1, 0:MC]), "sR")
    bld.step("sync", _g4)
    bld.flush()

    def _g5(e):
        bld.wait(e, "sR")
        bld.incd(e.dma_start(out=T["gin"][:], in_=sc["row"][1:2, 0:MC].partition_broadcast(P)), "sG")
        bld.incd(e.dma_start(out=T["gmc"][:], in_=sc["row"][3:4, 0:1].partition_broadcast(P)), "sG")
        bld.incd(e.dma_start(out=T["gam"][:, 1:2], in_=sc["row"][4:5, 0:1].partition_broadcast(P)), "sG")
        bld.incd(e.dma_start(out=T["gw"][:], in_=sc["row"][2:3, 0:MC].rearrange("o c -> (o c)").unsqueeze(1)), "sG")
    bld.step("gpsimd", _g5)
    bld.flush()

    def _g6(e):
        bld.wait(e, "sG", "sQ")
        inst = e.tensor_scalar_mul(T["MT"][:], T["MT"][:], T["gam"][:, 1:2])
        bld.inc(inst, "sV")
    bld.step("vector", _g6)
    bld.flush()

    def _g7(e):
        bld.wait(e, "sV")
        inst = e.activation(T["MT"][:], T["MT"][:], AF.Exp)
        bld.inc(inst, "sA")
    bld.step("scalar", _g7)
    bld.flush()

    with nc.sbuf_tensor("tcau", [P, MC], F32) as tcau:
        def _g9(e):
            bld.incd(e.dma_start(out=tcau[:], in_=il["mt_cau"]), "sQ")
        bld.step("sync", _g9)
        bld.flush()

        def _g10(e):
            bld.wait(e, "sQ", "sA", "sG")
            e.tensor_mul(T["MT"][:], T["MT"][:], tcau[:])
            inst = e.tensor_scalar_mul(T["gId"][:], T["idb"][:], T["gmc"][:])
            bld.inc(inst, "sV")
        bld.step("vector", _g10)
        bld.flush()

    def _a1(e):
        bld.wait(e, "sQ")
        inst = e.activation(T["A"][:], T["A"][:], AF.Exp)
        bld.inc(inst, "sA")
    bld.step("scalar", _a1)
    bld.flush()

    def _a2(e):
        bld.wait(e, "sA")
        inst = e.tensor_scalar_mul(T["A"][:], T["A"][:], -1.0)
        bld.inc(inst, "sV")
    bld.step("vector", _a2)
    bld.flush()

    if DBG_PREP[0] is not None:
        dbg = DBG_PREP[0]
        def _dp(e):
            bld.wait(e, "sV", "sA", "sG", "sQ")
            bld.incd(e.dma_start(out=dbg["mt"], in_=T["MT"][:]), "sG")
            bld.incd(e.dma_start(out=dbg["gam"], in_=T["gam"][:, 0:8]), "sG")
            bld.incd(e.dma_start(out=dbg["gin"], in_=T["gin"][:]), "sG")
            bld.incd(e.dma_start(out=dbg["gid"], in_=T["gId"][:]), "sG")
        bld.step("gpsimd", _dp)

    bld.bar()


def emit_attn(nc, bld, win, sc, il, T, A, x_cat, dbg):
    def _z(e):
        e.memset(A["st1"][:], 0.0)
        e.memset(A["st2"][:], 0.0)
        e.memset(A["cr1"][:], 0.0)
        e.memset(A["cr2"][:], 0.0)
        e.memset(A["o2c"][:], 0.0)
        inst = e.memset(A["Wt"][:], 0.0)
        bld.inc(inst, "sV")
    bld.step("vector", _z)

    for c in range(CK):
        emit_attn_chunk(nc, bld, win, sc, il, T, A, x_cat, dbg, c)


def mamba_pass(nc, bld, sc, il, T, A, c, mb):
    fcol = T["flag"][:, c:c + 1]
    state = A["st1"] if mb == 1 else A["st2"]
    carry = A["cr1"] if mb == 1 else A["cr2"]
    cbuf = c % 2

    def _fix(e):
        bld.wait(e, "sA", "sQ", "sG", "sL")
        xmp3 = A["xmp"][:].rearrange("p (g t) -> p g t", g=G)[:, :, 0:3]
        cr3 = carry[:].rearrange("p (g t) -> p g t", g=G)
        e.tensor_scalar_mul(xmp3, cr3, fcol)
        inst = e.tensor_scalar_mul(state[:], state[:], fcol)
        bld.inc(inst, "sV")
    bld.step("vector", _fix)

    q_dma = {}
    p_mm = {}
    a_ev = {}
    for dc in range(32):
        def _l(e, dc=dc):
            if dc >= 2:
                bld.wait_at(e, "sP", p_mm[dc - 2])
            buf = A["wbufA"] if dc % 2 == 0 else A["wbufB"]
            q_dma[dc] = bld.incd(e.dma_start(
                out=buf[:, 0:1024],
                in_=sc["inT"][dc].rearrange("p hc j -> p (hc j)")), "sQ")
        bld.step("sync", _l)

        def _m(e, dc=dc):
            bld.wait_at(e, "sQ", q_dma[dc])
            if dc >= 2:
                bld.wait_at(e, "sA", a_ev[dc - 2])
            if dc == 0:
                bld.wait(e, "sV")
            buf = A["wbufA"] if dc % 2 == 0 else A["wbufB"]
            ps = T["pA"] if dc % 2 == 0 else T["pB"]
            inst = None
            for hc in range(HC):
                inst = e.matmul(ps[:, 0:TC], buf[:, hc * P:(hc + 1) * P],
                                A["stB"][:, hc * TC:(hc + 1) * TC],
                                start=(hc == 0), stop=(hc == HC - 1))
            p_mm[dc] = bld.inc(inst, "sP")
        bld.step("tensor", _m)

        def _e(e, dc=dc):
            bld.wait_at(e, "sP", p_mm[dc])
            ps = T["pA"] if dc % 2 == 0 else T["pB"]
            if dc < G:
                dst = A["xmp"][:, dc * (TC + 3) + 3: dc * (TC + 3) + 3 + TC]
            else:
                dst = A["res"][:, (dc - G) * TC:(dc - G + 1) * TC]
            a_ev[dc] = bld.inc(e.activation(dst, ps[:, 0:TC], AF.Identity,
                                            bias=T["inb"][:, dc:dc + 1]), "sA")
        bld.step("scalar", _e)

    def _cv(e):
        bld.wait(e, "sA")
        xmp = A["xmp"]
        for g in range(G):
            base = g * (TC + 3)
            e.tensor_scalar_mul(A["cacc"][:, g * TC:(g + 1) * TC],
                                xmp[:, base:base + TC], T["cw"][:, g * 4:g * 4 + 1])
            for k in range(1, 4):
                e.scalar_tensor_tensor(A["cacc"][:, g * TC:(g + 1) * TC],
                                       xmp[:, base + k:base + k + TC],
                                       T["cw"][:, g * 4 + k:g * 4 + k + 1],
                                       A["cacc"][:, g * TC:(g + 1) * TC],
                                       ALU.mult, ALU.add)
        xmp3 = xmp[:].rearrange("p (g t) -> p g t", g=G)[:, :, TC:TC + 3]
        inst = e.tensor_copy(carry[:].rearrange("p (g t) -> p g t", g=G), xmp3)
        bld.inc(inst, "sV")
    bld.step("vector", _cv)

    def _si(e):
        bld.wait(e, "sV")
        for g in range(G):
            e.activation(A["xs"][:, g * TC:(g + 1) * TC], A["cacc"][:, g * TC:(g + 1) * TC],
                         AF.Silu, bias=T["cb"][:, g:g + 1])
        inst = None
        for g in range(G):
            inst = e.activation(A["res"][:, g * TC:(g + 1) * TC],
                                A["res"][:, g * TC:(g + 1) * TC], AF.Silu)
        bld.inc(inst, "sA")
    bld.step("scalar", _si)

    def _xp(e):
        bld.wait(e, "sA")
        inst = None
        for g in range(G):
            inst = e.matmul(T["pE"][0:96, 0:TC], T["xpT"][:, g * 96:(g + 1) * 96],
                            A["xs"][:, g * TC:(g + 1) * TC],
                            start=(g == 0), stop=(g == G - 1))
        bld.inc(inst, "sP")
    bld.step("tensor", _xp)

    def _xpe(e):
        bld.wait(e, "sP")
        e.activation(A["xdbl"][0:64, :], T["pE"][0:64, 0:TC], AF.Copy)
        inst = e.activation(A["xdbl"][64:96, :], T["pE"][64:96, 0:TC], AF.Copy)
        bld.inc(inst, "sA")
    bld.step("scalar", _xpe)

    def _bco(e):
        bld.wait(e, "sA")
        bld.incd(e.dma_start(out=sc["bc"][cbuf], in_=A["xdbl"][64:96, :]), "sR")
    bld.step("sync", _bco)

    def _bci(e):
        bld.wait(e, "sR", "sV", "sL")
        flat = sc["bc"][cbuf].rearrange("r t -> (r t)")
        bld.incd(e.dma_start(out=A["Bv"],
                             in_=flat[0:S * TC].unsqueeze(0).partition_broadcast(P)), "sQ")
        bld.incd(e.dma_start(out=A["xmp"][:, 0:S * TC],
                             in_=flat[S * TC:2 * S * TC].unsqueeze(0).partition_broadcast(P)), "sQ")
    bld.step("sync", _bci)

    for g in range(G):
        def _dt(e, g=g):
            bld.wait(e, "sA", "sV")
            inst = e.matmul(T["pF"][:, 0:TC], T["dtT"][:, g * P:(g + 1) * P],
                            A["xdbl"][0:64, :], start=True, stop=True)
            bld.inc(inst, "sP")
        bld.step("tensor", _dt)

        def _sp(e, g=g):
            bld.wait(e, "sP", "sV")
            dtb = T["dtb"][:, g:g + 1]
            e.activation(A["dsc"][:], T["pF"][:, 0:TC], AF.Abs, bias=dtb)
            e.activation(A["dsc"][:], A["dsc"][:], AF.Exp, scale=T["cst"][:, C_M1:C_M1 + 1])
            e.activation(A["dsc"][:], A["dsc"][:], AF.Ln, bias=1.0)
            inst = e.activation(A["delta"][:], T["pF"][:, 0:TC], AF.Relu, bias=dtb)
            bld.inc(inst, "sA")
        bld.step("scalar", _sp)

        def _dl(e, g=g):
            bld.wait(e, "sA", "sL")
            e.tensor_add(A["delta"][:], A["delta"][:], A["dsc"][:])
            inst = e.tensor_mul(A["yg"][:], A["delta"][:], A["xs"][:, g * TC:(g + 1) * TC])
            bld.inc(inst, "sV")
        bld.step("vector", _dl)

        def _dA(e, g=g):
            bld.wait(e, "sV", "sL")
            inst = None
            for si in range(S):
                inst = e.activation(A["dA"][:, si * TC:(si + 1) * TC], A["delta"][:],
                                    AF.Exp, scale=T["A"][:, g * S + si:g * S + si + 1])
            bld.inc(inst, "sA")
        bld.step("scalar", _dA)

        def _w(e, g=g):
            bld.wait(e, "sA", "sV", "sQ")
            inst = e.tensor_tensor(
                A["w"].rearrange("p (s t) -> p s t", s=S),
                bcast_hc(A["yg"][:], S, TC),
                A["Bv"].rearrange("p (s t) -> p s t", s=S),
                ALU.mult)
            bld.inc(inst, "sL")
        bld.step("vector", _w)

        def _scan(e, g=g):
            bld.wait(e, "sL", "sA")
            for si in range(S):
                e.tensor_tensor_scan(A["hg"][:, si * TC:(si + 1) * TC],
                                     A["dA"][:, si * TC:(si + 1) * TC],
                                     A["w"][:, si * TC:(si + 1) * TC],
                                     state[:, g * S + si:g * S + si + 1],
                                     ALU.mult, ALU.add)
            hv = A["hg"][:].rearrange("p (s t) -> p s t", s=S)[:, :, TC - 1:TC]
            inst = e.tensor_copy(state[:, g * S:(g + 1) * S].unsqueeze(2), hv)
            bld.inc(inst, "sV")
        bld.step("vector", _scan)

        def _cm(e, g=g):
            bld.wait(e, "sV")
            inst = e.tensor_tensor(
                A["w"].rearrange("p (s t) -> p s t", s=S),
                A["hg"][:].rearrange("p (s t) -> p s t", s=S),
                A["xmp"][:, 0:S * TC].rearrange("p (s t) -> p s t", s=S),
                ALU.mult)
            bld.inc(inst, "sL")
        bld.step("vector", _cm)

        def _tr(e, g=g):
            bld.wait(e, "sL")
            w = A["w"]
            e.tensor_add(w[:, 0:8 * TC], w[:, 0:8 * TC], w[:, 8 * TC:16 * TC])
            e.tensor_add(w[:, 0:4 * TC], w[:, 0:4 * TC], w[:, 4 * TC:8 * TC])
            e.tensor_add(w[:, 0:2 * TC], w[:, 0:2 * TC], w[:, 2 * TC:4 * TC])
            e.tensor_add(w[:, 0:TC], w[:, 0:TC], w[:, TC:2 * TC])
            e.scalar_tensor_tensor(A["yg"][:], A["xs"][:, g * TC:(g + 1) * TC],
                                   T["dss"][:, g:g + 1], w[:, 0:TC], ALU.mult, ALU.add)
            inst = e.tensor_mul(A["xs"][:, g * TC:(g + 1) * TC], A["yg"][:],
                                A["res"][:, g * TC:(g + 1) * TC])
            bld.inc(inst, "sV")
        bld.step("vector", _tr)

    for wv in range(2):
        qv = {}
        pv = {}
        for g in range(G):
            def _l(e, wv=wv, g=g):
                if g >= 2:
                    bld.wait_at(e, "sP", pv[g - 2])
                buf = A["wbufA"] if g % 2 == 0 else A["wbufB"]
                qv[g] = bld.incd(e.dma_start(
                    out=buf[:, 1024:1024 + 512],
                    in_=sc["outT"][g, :, wv * 4:(wv + 1) * 4, :]), "sQ")
            bld.step("sync", _l)

            def _m(e, wv=wv, g=g):
                bld.wait_at(e, "sQ", qv[g])
                if g == 0:
                    bld.wait(e, "sV", "sA")
                buf = A["wbufA"] if g % 2 == 0 else A["wbufB"]
                banks = [T["pA"], T["pB"], T["pC"], T["pD"]]
                inst = None
                for i in range(4):
                    inst = e.matmul(banks[i][:, 0:TC], buf[:, 1024 + i * P:1024 + (i + 1) * P],
                                    A["xs"][:, g * TC:(g + 1) * TC],
                                    start=(g == 0), stop=(g == G - 1))
                pv[g] = bld.inc(inst, "sP")
            bld.step("tensor", _m)

        def _ev(e, wv=wv):
            bld.wait(e, "sP")
            banks = [T["pA"], T["pB"], T["pC"], T["pD"]]
            inst = None
            for i in range(4):
                hc = wv * 4 + i
                if mb == 1:
                    dst = A["stC"][:, hc * TC:(hc + 1) * TC]
                else:
                    dst = A["o2e"][:, hc * (TC + 1) + 1: hc * (TC + 1) + 1 + TC]
                inst = e.activation(dst, banks[i][:, 0:TC], AF.Identity, bias=T["outb"][:, hc:hc + 1])
            bld.inc(inst, "sA")
        bld.step("scalar", _ev)


def emit_attn_chunk(nc, bld, win, sc, il, T, A, x_cat, dbg, c):
    bld.bar()
    fcol = T["flag"][:, c:c + 1]

    def _lx(e):
        inst = e.dma_start(out=A["xbf"][:].rearrange("p (hc t) -> p hc t", hc=HC),
                           in_=x_cat.rearrange("(hc p) T -> p hc T", p=P)[:, :, c * TC:(c + 1) * TC])
        bld.incd(inst, "sG")
    bld.step("gpsimd", _lx)

    def _wx(e):
        bld.wait(e, "sG")
        inst = e.engine_nop()
        bld.inc(inst, "sV")
    bld.step("vector", _wx)

    ln_stats(nc, bld, T, A["xbf"][:], HC, A["sq"], True)
    ln_apply(nc, bld, T, A["stA"][:], A["xbf"][:], HC, True)

    ln_stats(nc, bld, T, A["stA"][:], HC, A["sq"], False)
    ln_apply(nc, bld, T, A["stB"][:], A["stA"][:], HC, False, gains=(T["n1"], None))

    mamba_pass(nc, bld, sc, il, T, A, c, 1)

    def _x1(e):
        bld.wait(e, "sA", "sP")
        inst = None
        for hc in range(HC):
            inst = e.tensor_add(A["stA"][:, hc * TC:(hc + 1) * TC],
                                A["stA"][:, hc * TC:(hc + 1) * TC],
                                A["stC"][:, hc * TC:(hc + 1) * TC])
        bld.inc(inst, "sV")
    bld.step("vector", _x1)

    if "x1" in dbg:
        def _dbg(e):
            bld.wait(e, "sV")
            inst = e.dma_start(out=dbg["x1"].rearrange("(hc p) T -> p hc T", p=P)[:, :, c * TC:(c + 1) * TC],
                               in_=A["stA"][:].rearrange("p (hc t) -> p hc t", hc=HC))
            bld.incd(inst, "sG")
        bld.step("gpsimd", _dbg)

    if STAGE < 2:
        return

    ln_stats(nc, bld, T, A["stA"][:], HC, A["sq"], False)
    ln_apply(nc, bld, T, A["stB"][:], A["stA"][:], HC, False, gains=(T["n2"], None))

    def _o2f(e):
        bld.wait(e, "sV", "sA")
        o2v = A["o2e"][:].rearrange("p (hc t) -> p hc t", hc=HC)[:, :, 0:1]
        inst = e.tensor_scalar_mul(o2v, A["o2c"][:].unsqueeze(2), fcol)
        bld.inc(inst, "sV")
    bld.step("vector", _o2f)

    mamba_pass(nc, bld, sc, il, T, A, c, 2)

    if "o2" in dbg:
        def _dbgo2(e):
            bld.wait(e, "sA")
            inst = e.dma_start(out=dbg["o2"].rearrange("(hc p) T -> p hc T", p=P)[:, :, c * TC:(c + 1) * TC],
                               in_=A["o2e"][:].rearrange("p (hc t) -> p hc t", hc=HC)[:, :, 1:TC + 1])
            bld.incd(inst, "sG")
        bld.step("gpsimd", _dbgo2)

    def _o2s(e):
        bld.wait(e, "sA")
        o2v = A["o2e"][:].rearrange("p (hc t) -> p hc t", hc=HC)[:, :, TC:TC + 1]
        inst = e.tensor_copy(A["o2c"][:].unsqueeze(2), o2v)
        bld.inc(inst, "sV")
    bld.step("vector", _o2s)

    emit_mematt(nc, bld, sc, T, A, c, dbg)

    def _ao(e):
        bld.wait(e, "sV", "sA")
        inst = None
        for hc in range(HC):
            inst = e.tensor_add(A["stC"][:, hc * TC:(hc + 1) * TC],
                                A["stA"][:, hc * TC:(hc + 1) * TC],
                                A["stC"][:, hc * TC:(hc + 1) * TC])
        bld.inc(inst, "sV")
    bld.step("vector", _ao)

    if "ao" in dbg:
        def _dbg2(e):
            bld.wait(e, "sV")
            inst = e.dma_start(out=dbg["ao"].rearrange("(hc p) T -> p hc T", p=P)[:, :, c * TC:(c + 1) * TC],
                               in_=A["stC"][:].rearrange("p (hc t) -> p hc t", hc=HC))
            bld.incd(inst, "sG")
        bld.step("gpsimd", _dbg2)

    ln_stats(nc, bld, T, A["stC"][:], HC, A["sq"], True)
    ln_apply(nc, bld, T, A["stB"][:], A["stC"][:], HC, True)

    def _xA(e):
        bld.wait(e, "sV")
        inst = None
        for hc in range(HC):
            inst = e.tensor_add(A["stB"][:, hc * TC:(hc + 1) * TC],
                                A["xbf"][:, hc * TC:(hc + 1) * TC],
                                A["stB"][:, hc * TC:(hc + 1) * TC])
        bld.inc(inst, "sV")
    bld.step("vector", _xA)

    ln_stats(nc, bld, T, A["stB"][:], HC, A["sq"], True)
    ln_apply(nc, bld, T, A["stA"][:], A["stB"][:], HC, True)

    ln_stats(nc, bld, T, A["stA"][:], HC, A["sq"], True)
    ln_apply(nc, bld, T, A["stC"][:], A["stA"][:], HC, True, gains=(T["lng"], T["lnb"]))

    for name, tile in (("xA", A["stB"]), ("xo", A["stA"]), ("nx", A["stC"])):
        for m in range(NSUB):
            def _t(e, tile=tile, m=m):
                bld.wait(e, "sV", "sA", "sR")
                inst = None
                for hc in range(HC):
                    inst = e.transpose(T["pT"][:, hc * P:(hc + 1) * P],
                                       tile[:, hc * TC + m * P: hc * TC + (m + 1) * P], T["idb"][:])
                bld.inc(inst, "sP")
            bld.step("tensor", _t)

            def _e(e, m=m):
                bld.wait(e, "sP")
                inst = e.tensor_copy(A["tmt"], T["pT"][:, 0:D])
                bld.inc(inst, "sV")
            bld.step("vector", _e)

            def _s(e, name=name, m=m):
                bld.wait(e, "sV")
                bld.incd(e.dma_start(out=sc[name][c * TC + m * P: c * TC + (m + 1) * P, :],
                                     in_=A["tmt"]), "sR")
            bld.step("scalar", _s)

    if "nx" in dbg:
        def _dbg3(e):
            bld.wait(e, "sR")
            inst = e.dma_start(out=dbg["nx"][c * TC:(c + 1) * TC, :], in_=sc["nx"][c * TC:(c + 1) * TC, :])
            bld.incd(inst, "sG")
        bld.step("gpsimd", _dbg3)


def emit_mematt(nc, bld, sc, T, A, c, dbg=None):
    fcol = T["flag"][:, c:c + 1]

    def _wr(e):
        bld.wait(e, "sA")
        inst = e.tensor_scalar_mul(A["Wt"][:], A["Wt"][:], fcol)
        bld.inc(inst, "sV")
    bld.step("vector", _wr)

    qv = {}
    for hc in range(HC):
        def _l(e, hc=hc):
            bld.wait(e, "sP")
            buf = A["wbufA"] if hc % 2 == 0 else A["wbufB"]
            qv[hc] = bld.incd(e.dma_start(
                out=buf[:, 0:1024],
                in_=sc["wrT"][hc]), "sQ")
        bld.step("sync", _l)

        def _m(e, hc=hc):
            bld.wait_at(e, "sQ", qv[hc])
            bld.wait(e, "sV")
            buf = A["wbufA"] if hc % 2 == 0 else A["wbufB"]
            ps = T["pE"] if hc % 2 == 0 else T["pF"]
            inst = None
            for ec in range(HC):
                inst = e.matmul(ps[:, 0:TC], buf[:, ec * P:(ec + 1) * P],
                                A["o2e"][:, ec * (TC + 1) + 1: ec * (TC + 1) + 1 + TC],
                                start=(ec == 0), stop=(ec == HC - 1))
            bld.inc(inst, "sP")
        bld.step("tensor", _m)

        def _e(e, hc=hc):
            bld.wait(e, "sP", "sL")
            ps = T["pE"] if hc % 2 == 0 else T["pF"]
            inst = e.activation(A["vfm"][:, hc * TC:(hc + 1) * TC], ps[:, 0:TC], AF.Copy)
            bld.inc(inst, "sA")
        bld.step("scalar", _e)

    for m in range(NSUB):
        tok0 = m * MC

        def _tp(e, m=m, tok0=tok0):
            bld.wait(e, "sA", "sV")
            inst = None
            for hc in range(HC):
                inst = e.transpose(T["pT"][:, hc * P:(hc + 1) * P],
                                   A["vfm"][:, hc * TC + tok0: hc * TC + tok0 + MC], T["idb"][:])
            bld.inc(inst, "sP")
        bld.step("tensor", _tp)

        def _tpe(e):
            bld.wait(e, "sP")
            e.tensor_copy(A["vtm"], T["pT"][:, 0:D])
            inst = e.tensor_scalar_mul(A["vtm2"], A["vtm"], T["gw"][:])
            bld.inc(inst, "sV")
        bld.step("vector", _tpe)

        def _tp2(e, m=m, tok0=tok0):
            bld.wait(e, "sV")
            inst = None
            for hc in range(HC):
                inst = e.transpose(T["pT"][:, hc * P:(hc + 1) * P],
                                   A["o2e"][:, hc * (TC + 1) + tok0: hc * (TC + 1) + tok0 + MC], T["idb"][:])
            bld.inc(inst, "sP")
        bld.step("tensor", _tp2)

        def _tp2e(e):
            bld.wait(e, "sP")
            inst = e.tensor_copy(A["wktm"], T["pT"][:, 0:D])
            bld.inc(inst, "sV")
        bld.step("vector", _tp2e)

        def _rk(e, tok0=tok0):
            bld.wait(e, "sV")
            o2v = A["o2e"][:].rearrange("p (hc t) -> p hc t", hc=HC)[:, :, 1 + tok0:1 + tok0 + MC]
            inst = e.tensor_tensor(A["rksc"].rearrange("p (hc t) -> p hc t", hc=HC),
                                   o2v, bcast_hc(T["gin"][:], HC, MC), ALU.mult)
            bld.inc(inst, "sV")
        bld.step("vector", _rk)

        def _S(e, tok0=tok0):
            bld.wait(e, "sV")
            inst = None
            for hc in range(HC):
                inst = e.matmul(T["pG"][:, 0:MC],
                                A["o2e"][:, hc * (TC + 1) + tok0: hc * (TC + 1) + tok0 + MC],
                                A["o2e"][:, hc * (TC + 1) + 1 + tok0: hc * (TC + 1) + 1 + tok0 + MC],
                                start=(hc == 0), stop=(hc == HC - 1))
            bld.inc(inst, "sP")
        bld.step("tensor", _S)

        def _SM(e):
            bld.wait(e, "sP")
            inst = e.tensor_mul(A["SM"], T["pG"][:, 0:MC], T["MT"][:])
            bld.inc(inst, "sV")
        bld.step("vector", _SM)

        def _ii(e, m=m):
            bld.wait(e, "sV")
            inst = None
            for hcd in range(HC):
                bank = T["pE"] if hcd < 4 else T["pF"]
                off = (hcd % 4) * MC
                for ec in range(HC):
                    inst = e.matmul(bank[:, off:off + MC],
                                    A["Wt"][:, ec * D + hcd * P: ec * D + (hcd + 1) * P],
                                    A["rksc"][:, ec * MC:(ec + 1) * MC],
                                    start=(ec == 0), stop=False)
                inst = e.matmul(bank[:, off:off + MC],
                                A["vtm"][:, hcd * P:(hcd + 1) * P],
                                A["SM"], start=False, stop=True)
            bld.inc(inst, "sP")
        bld.step("tensor", _ii)

        def _re(e, m=m, tok0=tok0):
            bld.wait(e, "sP")
            rd = A["reads"].rearrange("p (hc t) -> p hc t", hc=HC)
            e.tensor_copy(rd[:, 0:4, tok0:tok0 + MC],
                          T["pE"][:, 0:4 * MC].rearrange("p (hc t) -> p hc t", hc=4))
            inst = e.tensor_copy(rd[:, 4:8, tok0:tok0 + MC],
                                 T["pF"][:, 0:4 * MC].rearrange("p (hc t) -> p hc t", hc=4))
            bld.inc(inst, "sV")
        bld.step("vector", _re)

        for ec in range(HC):
            def _wu(e, ec=ec):
                bld.wait(e, "sV", "sA")
                inst = None
                for h2 in range(2):
                    ps = T["pG"] if h2 == 0 else T["pC"]
                    e.matmul(ps[:, 0:512], T["gId"][:],
                             A["Wt"][:, ec * D + h2 * 512: ec * D + (h2 + 1) * 512],
                             start=True, stop=False)
                    inst = e.matmul(ps[:, 0:512], A["wktm"][:, ec * P:(ec + 1) * P],
                                    A["vtm2"][:, h2 * 512:(h2 + 1) * 512],
                                    start=False, stop=True)
                bld.inc(inst, "sP")
            bld.step("tensor", _wu)

            def _we(e, ec=ec):
                bld.wait(e, "sP")
                e.activation(A["Wt"][:, ec * D: ec * D + 512], T["pG"][:, 0:512], AF.Copy)
                inst = e.activation(A["Wt"][:, ec * D + 512: ec * D + 1024], T["pC"][:, 0:512], AF.Copy)
                bld.inc(inst, "sA")
            bld.step("scalar", _we)

    if dbg and "rdd" in dbg:
        def _dbgrd(e):
            bld.wait(e, "sV")
            inst = e.dma_start(out=dbg["rdd"].rearrange("(hc p) T -> p hc T", p=P)[:, :, c * TC:(c + 1) * TC],
                               in_=A["reads"].rearrange("p (hc t) -> p hc t", hc=HC))
            bld.incd(inst, "sG")
        bld.step("gpsimd", _dbgrd)

    qr = {}
    for hc in range(HC):
        def _l(e, hc=hc):
            bld.wait(e, "sP")
            buf = A["wbufA"] if hc % 2 == 0 else A["wbufB"]
            qr[hc] = bld.incd(e.dma_start(
                out=buf[:, 1024:2048],
                in_=sc["rdT"][hc]), "sQ")
        bld.step("sync", _l)

        def _m(e, hc=hc):
            bld.wait_at(e, "sQ", qr[hc])
            bld.wait(e, "sV")
            buf = A["wbufA"] if hc % 2 == 0 else A["wbufB"]
            ps = T["pE"] if hc % 2 == 0 else T["pF"]
            inst = None
            for dc in range(HC):
                inst = e.matmul(ps[:, 0:TC], buf[:, 1024 + dc * P:1024 + (dc + 1) * P],
                                A["reads"][:, dc * TC:(dc + 1) * TC],
                                start=(dc == 0), stop=(dc == HC - 1))
            bld.inc(inst, "sP")
        bld.step("tensor", _m)

        def _e(e, hc=hc):
            bld.wait(e, "sP")
            ps = T["pE"] if hc % 2 == 0 else T["pF"]
            inst = e.scalar_tensor_tensor(A["stC"][:, hc * TC:(hc + 1) * TC],
                                          ps[:, 0:TC], ALPHA,
                                          A["o2e"][:, hc * (TC + 1) + 1: hc * (TC + 1) + 1 + TC],
                                          ALU.mult, ALU.add)
            bld.inc(inst, "sV")
        bld.step("vector", _e)


def ffn_rows_bc(nc, bld, T, denom, tc=512):
    stat = T["stat"]

    def _rows(e):
        bld.wait(e, "sP")
        e.tensor_scalar_mul(stat[0:1, 1 * TC:1 * TC + tc], T["pC"][0:1, 0:tc], 1.0 / denom)
        e.tensor_scalar_mul(stat[0:1, 0 * TC:0 * TC + tc], T["pD"][0:1, 0:tc], 1.0 / denom)
        e.tensor_mul(stat[0:1, 2 * TC:2 * TC + tc], stat[0:1, 0 * TC:0 * TC + tc], stat[0:1, 0 * TC:0 * TC + tc])
        inst = e.tensor_sub(stat[0:1, 2 * TC:2 * TC + tc], stat[0:1, 1 * TC:1 * TC + tc], stat[0:1, 2 * TC:2 * TC + tc])
        bld.inc(inst, "sV")
    bld.step("vector", _rows)

    def _rstd(e):
        bld.wait(e, "sV")
        e.activation(stat[0:1, 3 * TC:3 * TC + tc], stat[0:1, 2 * TC:2 * TC + tc], AF.Ln,
                     bias=T["cst"][0:1, C_EPS:C_EPS + 1])
        e.activation(stat[0:1, 3 * TC:3 * TC + tc], stat[0:1, 3 * TC:3 * TC + tc], AF.Exp,
                     scale=T["cst"][0:1, C_MH:C_MH + 1])
        inst = e.activation(T["arow"][0:1, 0:tc], stat[0:1, 3 * TC:3 * TC + tc], AF.Copy)
        bld.inc(inst, "sA")
    bld.step("scalar", _rstd)

    def _b(e):
        bld.wait(e, "sA")
        e.tensor_mul(stat[0:1, 4 * TC:4 * TC + tc], stat[0:1, 0 * TC:0 * TC + tc], stat[0:1, 3 * TC:3 * TC + tc])
        inst = e.tensor_scalar_mul(T["brow"][0:1, 0:tc], stat[0:1, 4 * TC:4 * TC + tc], -1.0)
        bld.inc(inst, "sV")
    bld.step("vector", _b)

    def _bc(e):
        bld.wait(e, "sA", "sV")
        e.matmul(T["pD"][:, 0:tc], T["onesr"][:], T["arow"][0:1, 0:tc], start=True, stop=True)
        inst = e.matmul(T["pE"][:, 0:tc], T["onesr"][:], T["brow"][0:1, 0:tc], start=True, stop=True)
        bld.inc(inst, "sP")
    bld.step("tensor", _bc)


def gather_fm(nc, bld, T, Ff, src_cat, dst_ap, hf, jn=16):
    for j in range(jn):
        def _g(e, j=j):
            bld.wait(e, "sP")
            inst = e.indirect_dma_start(
                Ff["gat"][:], None,
                src_cat[0:P, :],
                IndirectOffsetOnAxis(Ff["idx"][:, hf * 16 + j:hf * 16 + j + 1], 0))
            bld.incd(inst, "sG")
        bld.step("gpsimd", _g)

        def _t(e, j=j):
            bld.wait(e, "sG", "sV")
            inst = None
            for hc in range(HC):
                inst = e.transpose(T["pT"][:, hc * P:(hc + 1) * P],
                                   Ff["gat"][:, hc * P:(hc + 1) * P], T["idb"][:])
            bld.inc(inst, "sP")
        bld.step("tensor", _t)

        def _e(e, j=j):
            bld.wait(e, "sP")
            inst = e.tensor_copy(
                dst_ap.rearrange("p (hc t) -> p hc t", hc=HC)[:, :, j * P:(j + 1) * P],
                T["pT"][:, 0:D].rearrange("p (hc t) -> p hc t", hc=HC))
            bld.inc(inst, "sV")
        bld.step("vector", _e)


def emit_ffn(nc, bld, win, sc, il, T, Ff, gidx, out_own, dbg):
    def _gi(e):
        ld = lambda d, src: bld.incd(e.dma_start(out=d, in_=src), "sQ")
        ld(Ff["idx"][:], gidx)
        ld(Ff["idf"][:].bitcast(BF16), il["id_f"].bitcast(BF16))
        ld(Ff["upb"][:].rearrange("p (a c) -> p a c", a=NE), win["upb_h"])
        ld(Ff["dwb"][:].rearrange("p (a c) -> p a c", a=NE), win["downb_h"])
        ld(Ff["s0b"][:].rearrange("p (a c) -> p a c", a=2), win["sp0b_h"])
        ld(Ff["l0g"][:].rearrange("p (a c) -> p a c", a=2), win["ln0g_h"])
        ld(Ff["l0b"][:].rearrange("p (a c) -> p a c", a=2), win["ln0b_h"])
        ld(Ff["s1ab"][:].rearrange("p (a c) -> p a c", a=2), win["sp1ab_h"])
        ld(Ff["s1bb"][:].rearrange("p (a c) -> p a c", a=2), win["sp1bb_h"])
        ld(Ff["l1g"][:].rearrange("p (a c) -> p a c", a=2), win["ln1g_h"])
        ld(Ff["l1b"][:].rearrange("p (a c) -> p a c", a=2), win["ln1b_h"])
        ld(Ff["s2b"][:].rearrange("p (a c) -> p a c", a=2), win["sp2b_h"])
    bld.step("sync", _gi)

    for hf in range(2):
        bld.bar()
        emit_ffn_half(nc, bld, win, sc, il, T, Ff, out_own, hf)
    bld.bar()


def _stream_stage(nc, bld, T, Ff, wsrc_fn, kn, nout, rhs_fn, evac_fn, tts=4):
    wst = Ff["wst"][:]
    for tt in range(tts):
        rhs_fn(tt)
        pq = {}
        for gc in range(nout):
            def _wl(e, gc=gc, tt=tt):
                if gc >= 2:
                    bld.wait_at(e, "sP", pq[gc - 2])
                buf = 0 if gc % 2 == 0 else 4096
                bld.incd(e.dma_start(out=wst[:, buf:buf + kn * P],
                                     in_=wsrc_fn(gc).rearrange("k p j -> p k j")), "sQ")
            bld.step("sync", _wl)

            def _mm(e, gc=gc, tt=tt):
                bld.wait(e, "sQ")
                bld.wait(e, "sA", "sV")
                buf = 0 if gc % 2 == 0 else 4096
                ps = T["pA"] if gc % 2 == 0 else T["pB"]
                inst = None
                for k in range(kn):
                    inst = e.matmul(ps[:, 0:512], wst[:, buf + k * P:buf + (k + 1) * P],
                                    Ff["rhs"][:, k * 512:(k + 1) * 512],
                                    start=(k == 0), stop=(k == kn - 1))
                pq[gc] = bld.inc(inst, "sP")
            bld.step("tensor", _mm)

            def _ev(e, gc=gc, tt=tt):
                bld.wait(e, "sP")
                ps = T["pA"] if gc % 2 == 0 else T["pB"]
                evac_fn(e, ps, gc, tt)
            bld.step("scalar", _ev)


def _load_rhs(nc, bld, Ff, sp, kn, tt):
    def _rl(e):
        bld.wait(e, "sP", "sV", "sR")
        for k in range(kn):
            bld.incd(e.dma_start(out=Ff["rhs"][:, k * 512:(k + 1) * 512],
                                 in_=sp[k][:, tt * 512:(tt + 1) * 512]), "sQ")
    bld.step("sync", _rl)
    if DEBUG_RHS_BAR[0]:
        bld.bar()


def emit_ffn_half(nc, bld, win, sc, il, T, Ff, out_own, hf):
    nxF = Ff["nxF"][:]
    acc = Ff["acc"][:]
    wst = Ff["wst"][:]

    gather_fm(nc, bld, T, Ff, sc["nx"], nxF, hf)

    def _z(e):
        bld.wait(e, "sV", "sA")
        inst = e.memset(acc, 0.0)
        bld.inc(inst, "sV")
    bld.step("vector", _z)

    for tt in range(4):
        def _lg(e, tt=tt):
            bld.wait(e, "sV", "sA")
            inst = None
            for hc in range(HC):
                inst = e.matmul(T["pE"][0:NE, 0:512], T["rwT"][:, hc * NE:(hc + 1) * NE],
                                nxF[:, hc * HALF + tt * 512:hc * HALF + (tt + 1) * 512],
                                start=(hc == 0), stop=(hc == HC - 1))
            bld.inc(inst, "sP")
        bld.step("tensor", _lg)

        def _le(e, tt=tt):
            bld.wait(e, "sP")
            inst = e.activation(Ff["lg"][0:NE, tt * 512:(tt + 1) * 512], T["pE"][0:NE, 0:512], AF.Copy)
            bld.inc(inst, "sA")
        bld.step("scalar", _le)

    for j in range(16):
        def _lt(e, j=j):
            bld.wait(e, "sA", "sV")
            inst = e.transpose(T["pF"][0:P, 0:NE], Ff["lg"][0:NE, j * P:(j + 1) * P],
                               Ff["idf"][0:NE, 0:NE])
            bld.inc(inst, "sP")
        bld.step("tensor", _lt)

        def _lte(e, j=j):
            bld.wait(e, "sP")
            inst = e.tensor_copy(Ff["ptm"][:, j * NE:(j + 1) * NE], T["pF"][0:P, 0:NE])
            bld.inc(inst, "sV")
        bld.step("vector", _lte)

    def _sm1(e):
        bld.wait(e, "sV")
        p3 = Ff["ptm"][:].rearrange("p (j e) -> p j e", j=16)
        m3 = Ff["m1"][:].unsqueeze(2)
        e.tensor_reduce(m3, p3, AX.X, ALU.max)
        e.tensor_tensor(Ff["msk"][:].rearrange("p (j e) -> p j e", j=16), p3,
                        m3.broadcast_to([P, 16, NE]), ALU.subtract)
        e.tensor_tensor(Ff["w2"][:].rearrange("p (j e) -> p j e", j=16), p3,
                        m3.broadcast_to([P, 16, NE]), ALU.is_ge)
        e.scalar_tensor_tensor(Ff["w2"][:], Ff["w2"][:], -1e9, Ff["ptm"][:], ALU.mult, ALU.add)
        w23 = Ff["w2"][:].rearrange("p (j e) -> p j e", j=16)
        e.tensor_reduce(m3, w23, AX.X, ALU.max)
        inst = e.tensor_tensor(w23, p3, m3.broadcast_to([P, 16, NE]), ALU.is_ge)
        bld.inc(inst, "sV")
    bld.step("vector", _sm1)

    def _sm2(e):
        bld.wait(e, "sV")
        inst = e.activation(Ff["msk"][:], Ff["msk"][:], AF.Exp)
        bld.inc(inst, "sA")
    bld.step("scalar", _sm2)

    def _sm3(e):
        bld.wait(e, "sA")
        ex3 = Ff["msk"][:].rearrange("p (j e) -> p j e", j=16)
        w23 = Ff["w2"][:].rearrange("p (j e) -> p j e", j=16)
        s3 = Ff["s1"][:].unsqueeze(2)
        e.tensor_tensor(ex3, ex3, w23, ALU.mult)
        inst = e.tensor_reduce(s3, ex3, AX.X, ALU.add)
        bld.inc(inst, "sV")
    bld.step("vector", _sm3)

    def _sm4(e):
        bld.wait(e, "sV")
        e.activation(Ff["s2"][:], Ff["s1"][:], AF.Ln)
        inst = e.activation(Ff["s2"][:], Ff["s2"][:], AF.Exp, scale=T["cst"][:, C_M1:C_M1 + 1])
        bld.inc(inst, "sA")
    bld.step("scalar", _sm4)

    def _sm5(e):
        bld.wait(e, "sA")
        ex3 = Ff["msk"][:].rearrange("p (j e) -> p j e", j=16)
        inst = e.tensor_tensor(ex3, ex3, Ff["s2"][:].unsqueeze(2).broadcast_to([P, 16, NE]), ALU.mult)
        bld.inc(inst, "sV")
    bld.step("vector", _sm5)

    for j in range(16):
        def _wt(e, j=j):
            bld.wait(e, "sV")
            inst = e.transpose(T["pE"][0:NE, 0:P], Ff["msk"][:, j * NE:(j + 1) * NE], Ff["idf"][:])
            bld.inc(inst, "sP")
        bld.step("tensor", _wt)

        def _wte(e, j=j):
            bld.wait(e, "sP")
            inst = e.tensor_copy(Ff["w8"][0:NE, j * P:(j + 1) * P], T["pE"][0:NE, 0:P])
            bld.inc(inst, "sV")
        bld.step("vector", _wte)

    def _wout(e):
        bld.wait(e, "sV")
        bld.incd(e.dma_start(out=sc["wrt"], in_=Ff["w8"][:]), "sR")
    bld.step("scalar", _wout)

    for ee in range(NE):
        j2 = ee // 4
        kind = ee % 4

        pq = {}
        aq = {}
        for fc in range(FC):
            def _wl(e, fc=fc, ee=ee):
                if fc >= 2:
                    bld.wait_at(e, "sP", pq[fc - 2])
                else:
                    bld.wait(e, "sP")
                buf = 0 if fc % 2 == 0 else 4096
                bld.incd(e.dma_start(out=wst[:, buf:buf + HC * P],
                                     in_=sc["up"][ee, fc]), "sQ")
            bld.step("sync", _wl)

            def _mm(e, fc=fc):
                bld.wait(e, "sQ")
                if fc >= 1:
                    bld.wait_at(e, "sA", aq[fc - 1])
                else:
                    bld.wait(e, "sA", "sV")
                buf = 0 if fc % 2 == 0 else 4096
                banks = [T["pA"], T["pB"], T["pC"], T["pD"]]
                inst = None
                for tt in range(4):
                    for hc in range(HC):
                        inst = e.matmul(banks[tt][:, 0:512], wst[:, buf + hc * P:buf + (hc + 1) * P],
                                        nxF[:, hc * HALF + tt * 512:hc * HALF + (tt + 1) * 512],
                                        start=(hc == 0), stop=(hc == HC - 1))
                pq[fc] = bld.inc(inst, "sP")
            bld.step("tensor", _mm)

            def _ev(e, fc=fc, ee=ee):
                bld.wait(e, "sP", "sR")
                banks = [T["pA"], T["pB"], T["pC"], T["pD"]]
                evs = [Ff["ev"], Ff["ev2"], Ff["ev3"], Ff["ev4"]]
                for tt in range(4):
                    inst = e.activation(evs[tt][:], banks[tt][:, 0:512], AF.Gelu,
                                        bias=Ff["upb"][:, ee * FC + fc:ee * FC + fc + 1])
                    n = bld.inc(inst, "sA")
                    if bld.mode == "plan":
                        aq[fc] = n
            bld.step("scalar", _ev)

            def _sp(e, fc=fc):
                bld.wait(e, "sA")
                evs = [Ff["ev"], Ff["ev2"], Ff["ev3"], Ff["ev4"]]
                for tt in range(4):
                    bld.incd(e.dma_start(out=sc["h1"][fc][:, tt * 512:(tt + 1) * 512],
                                         in_=evs[tt][:]), "sR")
            bld.step("gpsimd", _sp)

        if EMIT_DBG[0] is not None and ee == 6:
            dbg = EMIT_DBG[0]
            def _h1d(e):
                bld.wait(e, "sV", "sA", "sR", "sP", "sQ", "sG")
                bld.incd(e.dma_start(out=dbg["h1"].rearrange("(a p) t -> a p t", p=P),
                                     in_=sc["h1"]), "sG")
            bld.step("gpsimd", _h1d)
            bld.bar()

        if kind == 0 or kind == 2:
            wsc = sc["sp0"] if kind == 0 else sc["sp2"]
            bias = Ff["s0b"] if kind == 0 else Ff["s2b"]
            with_ln = (kind == 0)
            for tt in range(4):
                _load_rhs(nc, bld, Ff, sc["h1"], FC, tt)
                pq2 = {}
                for gc in range(FC):
                    def _wl2(e, gc=gc, tt=tt, wsc=wsc, j2=j2):
                        if gc >= 2:
                            bld.wait_at(e, "sP", pq2[gc - 2])
                        else:
                            bld.wait(e, "sP")
                        buf = 0 if gc % 2 == 0 else 4096
                        bld.incd(e.dma_start(out=wst[:, buf:buf + FC * P],
                                             in_=wsc[j2, gc]), "sQ")
                    bld.step("sync", _wl2)

                    def _mm2(e, gc=gc, tt=tt):
                        bld.wait(e, "sQ", "sA")
                        buf = 0 if gc % 2 == 0 else 4096
                        ps = T["pE"] if gc % 2 == 0 else T["pF"]
                        inst = None
                        for k in range(FC):
                            inst = e.matmul(ps[:, 0:512], wst[:, buf + k * P:buf + (k + 1) * P],
                                            Ff["rhs"][:, k * 512:(k + 1) * 512],
                                            start=(k == 0), stop=(k == FC - 1))
                        pq2[gc] = bld.inc(inst, "sP")
                    bld.step("tensor", _mm2)

                    if with_ln:
                        def _ev2(e, gc=gc, tt=tt, j2=j2, bias=bias):
                            bld.wait(e, "sP", "sR")
                            ps = T["pE"] if gc % 2 == 0 else T["pF"]
                            e.activation(Ff["ev"][:], ps[:, 0:512], AF.Identity,
                                         bias=bias[:, j2 * FC + gc:j2 * FC + gc + 1])
                            inst = e.activation(Ff["sq"][:], Ff["ev"][:], AF.Square)
                            bld.inc(inst, "sA")
                        bld.step("scalar", _ev2)

                        def _sp2(e, gc=gc, tt=tt):
                            bld.wait(e, "sA")
                            bld.incd(e.dma_start(out=sc["spp"][gc][:, tt * 512:(tt + 1) * 512],
                                                 in_=Ff["ev"][:]), "sR")
                        bld.step("gpsimd", _sp2)

                        def _st2(e, gc=gc):
                            bld.wait(e, "sA")
                            e.matmul(T["pC"][0:1, 0:512], T["ones"][:], Ff["sq"][:],
                                     start=(gc == 0), stop=(gc == FC - 1))
                            inst = e.matmul(T["pD"][0:1, 0:512], T["ones"][:], Ff["ev"][:],
                                            start=(gc == 0), stop=(gc == FC - 1))
                            bld.inc(inst, "sP")
                        bld.step("tensor", _st2)
                    else:
                        def _ev2(e, gc=gc, tt=tt, j2=j2, bias=bias):
                            bld.wait(e, "sP", "sR")
                            ps = T["pE"] if gc % 2 == 0 else T["pF"]
                            inst = e.activation(Ff["ev"][:], ps[:, 0:512], AF.Gelu,
                                                bias=bias[:, j2 * FC + gc:j2 * FC + gc + 1])
                            bld.inc(inst, "sA")
                        bld.step("scalar", _ev2)

                        def _sp2(e, gc=gc, tt=tt):
                            bld.wait(e, "sA")
                            bld.incd(e.dma_start(out=sc["h2"][gc][:, tt * 512:(tt + 1) * 512],
                                                 in_=Ff["ev"][:]), "sR")
                        bld.step("gpsimd", _sp2)

                if with_ln:
                    ffn_rows_bc(nc, bld, T, FF)
                    for gc in range(FC):
                        def _al(e, gc=gc, tt=tt):
                            bld.wait(e, "sV", "sA", "sR")
                            bld.incd(e.dma_start(out=Ff["ev"][:],
                                                 in_=sc["spp"][gc][:, tt * 512:(tt + 1) * 512]), "sQ")
                        bld.step("sync", _al)

                        def _ap2(e, gc=gc):
                            bld.wait(e, "sQ", "sA", "sP")
                            e.tensor_mul(Ff["evf"][:], Ff["ev"][:], T["pD"][:, 0:512])
                            inst = e.tensor_add(Ff["evf"][:], Ff["evf"][:], T["pE"][:, 0:512])
                            bld.inc(inst, "sV")
                        bld.step("vector", _ap2)

                        def _gl(e, gc=gc, tt=tt, j2=j2):
                            bld.wait(e, "sV", "sR")
                            inst = e.activation(Ff["ev2"][:], Ff["evf"][:], AF.Gelu,
                                                scale=Ff["l0g"][:, j2 * FC + gc:j2 * FC + gc + 1],
                                                bias=Ff["l0b"][:, j2 * FC + gc:j2 * FC + gc + 1])
                            bld.inc(inst, "sA")
                        bld.step("scalar", _gl)

                        def _gls(e, gc=gc, tt=tt):
                            bld.wait(e, "sA")
                            bld.incd(e.dma_start(out=sc["h2"][gc][:, tt * 512:(tt + 1) * 512],
                                                 in_=Ff["ev2"][:]), "sR")
                        bld.step("gpsimd", _gls)

        elif kind == 1:
            for tt in range(4):
                _load_rhs(nc, bld, Ff, sc["h1"], FC, tt)
                pq2 = {}
                for gc in range(F2C):
                    def _wl2(e, gc=gc, tt=tt, j2=j2):
                        if gc >= 2:
                            bld.wait_at(e, "sP", pq2[gc - 2])
                        else:
                            bld.wait(e, "sP")
                        buf = 0 if gc % 2 == 0 else 4096
                        bld.incd(e.dma_start(out=wst[:, buf:buf + FC * P],
                                             in_=sc["sp1a"][j2, gc]), "sQ")
                    bld.step("sync", _wl2)

                    def _mm2(e, gc=gc):
                        bld.wait(e, "sQ", "sA")
                        buf = 0 if gc % 2 == 0 else 4096
                        ps = T["pE"] if gc % 2 == 0 else T["pF"]
                        inst = None
                        for k in range(FC):
                            inst = e.matmul(ps[:, 0:512], wst[:, buf + k * P:buf + (k + 1) * P],
                                            Ff["rhs"][:, k * 512:(k + 1) * 512],
                                            start=(k == 0), stop=(k == FC - 1))
                        pq2[gc] = bld.inc(inst, "sP")
                    bld.step("tensor", _mm2)

                    def _ev2(e, gc=gc, tt=tt, j2=j2):
                        bld.wait(e, "sP", "sR")
                        ps = T["pE"] if gc % 2 == 0 else T["pF"]
                        inst = e.activation(Ff["ev"][:], ps[:, 0:512], AF.Gelu,
                                            bias=Ff["s1ab"][:, j2 * F2C + gc:j2 * F2C + gc + 1])
                        bld.inc(inst, "sA")
                    bld.step("scalar", _ev2)

                    def _sp2b(e, gc=gc, tt=tt):
                        bld.wait(e, "sA")
                        bld.incd(e.dma_start(out=sc["t1"][gc][:, tt * 512:(tt + 1) * 512],
                                             in_=Ff["ev"][:]), "sR")
                    bld.step("gpsimd", _sp2b)
            for tt in range(4):
                _load_rhs(nc, bld, Ff, sc["t1"], F2C, tt)
                pq2 = {}
                for gc in range(FC):
                    def _wl2(e, gc=gc, tt=tt, j2=j2):
                        if gc >= 2:
                            bld.wait_at(e, "sP", pq2[gc - 2])
                        else:
                            bld.wait(e, "sP")
                        buf = 0 if gc % 2 == 0 else 4096
                        bld.incd(e.dma_start(out=wst[:, buf:buf + F2C * P],
                                             in_=sc["sp1b"][j2, gc]), "sQ")
                    bld.step("sync", _wl2)

                    def _mm2(e, gc=gc):
                        bld.wait(e, "sQ", "sA")
                        buf = 0 if gc % 2 == 0 else 4096
                        ps = T["pE"] if gc % 2 == 0 else T["pF"]
                        inst = None
                        for k in range(F2C):
                            inst = e.matmul(ps[:, 0:512], wst[:, buf + k * P:buf + (k + 1) * P],
                                            Ff["rhs"][:, k * 512:(k + 1) * 512],
                                            start=(k == 0), stop=(k == F2C - 1))
                        pq2[gc] = bld.inc(inst, "sP")
                    bld.step("tensor", _mm2)

                    def _ev2(e, gc=gc, tt=tt, j2=j2):
                        bld.wait(e, "sP", "sR")
                        ps = T["pE"] if gc % 2 == 0 else T["pF"]
                        e.activation(Ff["ev"][:], ps[:, 0:512], AF.Identity,
                                     bias=Ff["s1bb"][:, j2 * FC + gc:j2 * FC + gc + 1])
                        inst = e.activation(Ff["sq"][:], Ff["ev"][:], AF.Square)
                        bld.inc(inst, "sA")
                    bld.step("scalar", _ev2)

                    def _sp2c(e, gc=gc, tt=tt):
                        bld.wait(e, "sA")
                        bld.incd(e.dma_start(out=sc["spp"][gc][:, tt * 512:(tt + 1) * 512],
                                             in_=Ff["ev"][:]), "sR")
                    bld.step("gpsimd", _sp2c)

                    def _st2(e, gc=gc):
                        bld.wait(e, "sA")
                        e.matmul(T["pC"][0:1, 0:512], T["ones"][:], Ff["sq"][:],
                                 start=(gc == 0), stop=(gc == FC - 1))
                        inst = e.matmul(T["pD"][0:1, 0:512], T["ones"][:], Ff["ev"][:],
                                        start=(gc == 0), stop=(gc == FC - 1))
                        bld.inc(inst, "sP")
                    bld.step("tensor", _st2)

                ffn_rows_bc(nc, bld, T, FF)
                for gc in range(FC):
                    def _al(e, gc=gc, tt=tt):
                        bld.wait(e, "sV", "sA", "sR")
                        bld.incd(e.dma_start(out=Ff["ev"][:],
                                             in_=sc["spp"][gc][:, tt * 512:(tt + 1) * 512]), "sQ")
                    bld.step("sync", _al)

                    def _ap2(e, gc=gc):
                        bld.wait(e, "sQ", "sA", "sP")
                        e.tensor_mul(Ff["evf"][:], Ff["ev"][:], T["pD"][:, 0:512])
                        inst = e.tensor_add(Ff["evf"][:], Ff["evf"][:], T["pE"][:, 0:512])
                        bld.inc(inst, "sV")
                    bld.step("vector", _ap2)

                    def _gl(e, gc=gc, tt=tt, j2=j2):
                        bld.wait(e, "sV", "sR")
                        inst = e.activation(Ff["ev2"][:], Ff["evf"][:], AF.Identity,
                                            scale=Ff["l1g"][:, j2 * FC + gc:j2 * FC + gc + 1],
                                            bias=Ff["l1b"][:, j2 * FC + gc:j2 * FC + gc + 1])
                        bld.inc(inst, "sA")
                    bld.step("scalar", _gl)

                    def _gls(e, gc=gc, tt=tt):
                        bld.wait(e, "sA")
                        bld.incd(e.dma_start(out=sc["h2"][gc][:, tt * 512:(tt + 1) * 512],
                                             in_=Ff["ev2"][:]), "sR")
                    bld.step("gpsimd", _gls)

        hsrc = sc["h1"] if kind == 3 else sc["h2"]

        def _wbc(e, ee=ee):
            bld.wait(e, "sR", "sV")
            bld.incd(e.dma_start(out=Ff["wbc"][:],
                                 in_=sc["wrt"][ee:ee + 1, :].partition_broadcast(P)), "sG")
        bld.step("gpsimd", _wbc)

        for tt in range(4):
            _load_rhs(nc, bld, Ff, hsrc, FC, tt)
            pq3 = {}
            for hc in range(HC):
                def _wl3(e, hc=hc, tt=tt, ee=ee):
                    if hc >= 2:
                        bld.wait_at(e, "sP", pq3[hc - 2])
                    else:
                        bld.wait(e, "sP")
                    buf = 0 if hc % 2 == 0 else 4096
                    bld.incd(e.dma_start(out=wst[:, buf:buf + FC * P],
                                         in_=sc["down"][ee, hc]), "sQ")
                bld.step("sync", _wl3)

                def _mm3(e, hc=hc):
                    bld.wait(e, "sQ", "sA")
                    buf = 0 if hc % 2 == 0 else 4096
                    ps = T["pE"] if hc % 2 == 0 else T["pF"]
                    inst = None
                    for k in range(FC):
                        inst = e.matmul(ps[:, 0:512], wst[:, buf + k * P:buf + (k + 1) * P],
                                        Ff["rhs"][:, k * 512:(k + 1) * 512],
                                        start=(k == 0), stop=(k == FC - 1))
                    pq3[hc] = bld.inc(inst, "sP")
                bld.step("tensor", _mm3)

                def _ev3(e, hc=hc, tt=tt, ee=ee):
                    bld.wait(e, "sP", "sV")
                    ps = T["pE"] if hc % 2 == 0 else T["pF"]
                    inst = e.activation(Ff["ev"][:], ps[:, 0:512], AF.Identity,
                                        bias=Ff["dwb"][:, ee * HC + hc:ee * HC + hc + 1])
                    bld.inc(inst, "sA")
                bld.step("scalar", _ev3)

                def _cb3(e, hc=hc, tt=tt):
                    bld.wait(e, "sA", "sG")
                    e.tensor_mul(Ff["evf"][:], Ff["ev"][:], Ff["wbc"][:, tt * 512:(tt + 1) * 512])
                    sl = acc[:, hc * HALF + tt * 512:hc * HALF + (tt + 1) * 512]
                    inst = e.tensor_add(sl, sl, Ff["evf"][:])
                    bld.inc(inst, "sV")
                bld.step("vector", _cb3)

    if hf == 1 and EMIT_DBG[0] is not None:
        dbg = EMIT_DBG[0]
        def _fd(e):
            bld.wait(e, "sV", "sA", "sR", "sP")
            bld.incd(e.dma_start(out=dbg["acc"], in_=acc), "sG")
            bld.incd(e.dma_start(out=dbg["h1"].rearrange("(a p) t -> a p t", p=P), in_=sc["h1"]), "sG")
            bld.incd(e.dma_start(out=dbg["h2"].rearrange("(a p) t -> a p t", p=P), in_=sc["h2"]), "sG")
            bld.incd(e.dma_start(out=dbg["wrt"], in_=sc["wrt"]), "sG")
        bld.step("gpsimd", _fd)

    bld.bar()
    gather_fm(nc, bld, T, Ff, sc["xo"], Ff["rhs"][:, 0:HC * HALF], hf)

    def _yadd(e):
        bld.wait(e, "sV")
        inst = e.tensor_add(acc, acc, Ff["rhs"][:, 0:HC * HALF])
        bld.inc(inst, "sV")
    bld.step("vector", _yadd)

    bld.bar()
    gather_fm(nc, bld, T, Ff, sc["xA"], nxF, hf)

    ybf = wst[:, 0:4096]
    sqw = wst[:, 4096:8192]
    for tt in range(4):
        def _yb(e, tt=tt):
            bld.wait(e, "sV", "sP")
            inst = None
            for hc in range(HC):
                inst = e.activation(ybf[:, hc * 512:(hc + 1) * 512],
                                    acc[:, hc * HALF + tt * 512:hc * HALF + (tt + 1) * 512], AF.Copy)
            inst2 = e.activation(sqw[:], ybf[:], AF.Square)
            bld.inc(inst2, "sA")
        bld.step("scalar", _yb)

        def _ymm(e, tt=tt):
            bld.wait(e, "sA")
            inst = None
            for hc in range(HC):
                inst = e.matmul(T["pC"][0:1, 0:512], T["ones"][:], sqw[:, hc * 512:(hc + 1) * 512],
                                start=(hc == 0), stop=(hc == HC - 1))
            for hc in range(HC):
                inst = e.matmul(T["pD"][0:1, 0:512], T["ones"][:], ybf[:, hc * 512:(hc + 1) * 512],
                                start=(hc == 0), stop=(hc == HC - 1))
            bld.inc(inst, "sP")
        bld.step("tensor", _ymm)

        ffn_rows_bc(nc, bld, T, D)

        def _yap(e, tt=tt):
            bld.wait(e, "sP")
            inst = None
            for hc in range(HC):
                sl = acc[:, hc * HALF + tt * 512:hc * HALF + (tt + 1) * 512]
                e.tensor_mul(sl, ybf[:, hc * 512:(hc + 1) * 512], T["pD"][:, 0:512])
                e.tensor_add(sl, sl, T["pE"][:, 0:512])
                inst = e.tensor_add(sl, sl, nxF[:, hc * HALF + tt * 512:hc * HALF + (tt + 1) * 512])
            bld.inc(inst, "sV")
        bld.step("vector", _yap)

        for j in range(4):
            def _ot(e, tt=tt, j=j):
                bld.wait(e, "sV", "sR")
                inst = None
                for hc in range(HC):
                    bank = T["pF"] if hc < 4 else T["pG"]
                    inst = e.transpose(bank[:, (hc % 4) * P:(hc % 4 + 1) * P],
                                       acc[:, hc * HALF + tt * 512 + j * P:hc * HALF + tt * 512 + (j + 1) * P],
                                       Ff["idf"][:])
                bld.inc(inst, "sP")
            bld.step("tensor", _ot)

            def _oe(e, tt=tt, j=j):
                bld.wait(e, "sP")
                e.tensor_copy(Ff["outs"][:, 0:512], T["pF"][:, 0:512])
                inst = e.tensor_copy(Ff["outs"][:, 512:1024], T["pG"][:, 0:512])
                bld.inc(inst, "sV")
            bld.step("vector", _oe)

            def _os(e, tt=tt, j=j):
                bld.wait(e, "sV")
                row0 = hf * HALF + tt * 512 + j * P
                bld.incd(e.dma_start(out=out_own[row0:row0 + P, :], in_=Ff["outs"][:]), "sR")
            bld.step("scalar", _os)


def _prep_core_inputs(inputs, k):
    x = np.asarray(inputs["L0_attn_x"])
    span0 = (k + 1) * 2048
    span1 = (8 - k) * 2048
    xcat = np.concatenate([x[0, :span0], x[1, :span1]], axis=0)
    xcat_fm = np.ascontiguousarray(xcat.T)
    jb = span0 // TC
    flags = np.ones(CK, np.float32)
    flags[0] = 0.0
    flags[jb] = 0.0
    flags_b = np.tile(flags.reshape(1, CK), (P, 1)).astype(np.float32)
    own0 = np.arange(span0 - 2048, span0, dtype=np.int32)
    own1 = np.arange(CK * TC - 2048, CK * TC, dtype=np.int32)
    gidx = np.ascontiguousarray(np.concatenate([own0, own1]).reshape(32, P).T)
    return xcat_fm, flags_b, gidx


def _tile_weights(inputs):
    f32 = lambda a: np.ascontiguousarray(np.asarray(a), dtype=np.float32)
    g = lambda n: f32(inputs[n])
    w = {}
    in_w = g("L0_attn_in_proj_w")
    w["inT_h"] = np.ascontiguousarray(in_w.reshape(32, P, HC, P).transpose(0, 3, 2, 1))
    out_w = g("L0_attn_out_proj_w")
    w["outT_h"] = np.ascontiguousarray(out_w.reshape(HC, P, G, P).transpose(2, 3, 0, 1))
    wr = g("L0_attn_proj_write_w")
    w["wrT_h"] = np.ascontiguousarray(wr.reshape(HC, P, HC, P).transpose(0, 3, 2, 1))
    rd = g("L0_attn_proj_read_w")
    w["rdT_h"] = np.ascontiguousarray(rd.reshape(HC, P, HC, P).transpose(0, 3, 2, 1))
    w["dtT_h"] = np.ascontiguousarray(g("L0_attn_dt_proj_w").T)
    w["xpT_h"] = np.ascontiguousarray(g("L0_attn_x_proj_w").reshape(96, G, P).transpose(2, 1, 0))
    w["rw_h"] = np.ascontiguousarray(g("L0_ffn_router_w").reshape(HC, P, NE).transpose(1, 0, 2))
    w["up_h"] = np.ascontiguousarray(g("L0_ffn_up_W").reshape(NE, HC, P, FC, P).transpose(0, 3, 2, 1, 4))
    w["down_h"] = np.ascontiguousarray(g("L0_ffn_down_W").reshape(NE, FC, P, HC, P).transpose(0, 3, 2, 1, 4))
    w["sp0_h"] = np.ascontiguousarray(g("L0_ffn_spec0_W").reshape(2, FC, P, FC, P).transpose(0, 3, 2, 1, 4))
    w["sp1a_h"] = np.ascontiguousarray(g("L0_ffn_spec1a_W").reshape(2, FC, P, F2C, P).transpose(0, 3, 2, 1, 4))
    w["sp1b_h"] = np.ascontiguousarray(g("L0_ffn_spec1b_W").reshape(2, F2C, P, FC, P).transpose(0, 3, 2, 1, 4))
    w["sp2_h"] = np.ascontiguousarray(g("L0_ffn_spec2_W").reshape(2, FC, P, FC, P).transpose(0, 3, 2, 1, 4))
    w["A_h"] = np.ascontiguousarray(g("L0_attn_A_log").reshape(G, P, S).transpose(1, 0, 2))
    w["cw_h"] = np.ascontiguousarray(g("L0_attn_conv_w").reshape(G, P, 4).transpose(1, 0, 2))
    colmaj = lambda a, n: np.ascontiguousarray(f32(a).reshape(n, P).T)
    w["cb_h"] = colmaj(inputs["L0_attn_conv_b"], G)
    w["dtb_h"] = colmaj(inputs["L0_attn_dt_proj_b"], G)
    w["dss_h"] = colmaj(inputs["L0_attn_D_ssm"], G)
    w["inb_h"] = colmaj(inputs["L0_attn_in_proj_b"], 32)
    w["outb_h"] = colmaj(inputs["L0_attn_out_proj_b"], HC)
    w["n1_h"] = colmaj(inputs["L0_attn_norm1_w"], HC)
    w["n2_h"] = colmaj(inputs["L0_attn_norm2_w"], HC)
    w["lng_h"] = colmaj(inputs["L0_ffn_ln_g"], HC)
    w["lnb_h"] = colmaj(inputs["L0_ffn_ln_b"], HC)
    pm = lambda a, e, n: np.ascontiguousarray(f32(a).reshape(e, n, P).transpose(2, 0, 1))
    w["upb_h"] = pm(inputs["L0_ffn_up_b"], NE, FC)
    w["downb_h"] = pm(inputs["L0_ffn_down_b"], NE, HC)
    w["sp0b_h"] = pm(inputs["L0_ffn_spec0_b"], 2, FC)
    w["ln0g_h"] = pm(inputs["L0_ffn_ln0_g"], 2, FC)
    w["ln0b_h"] = pm(inputs["L0_ffn_ln0_b"], 2, FC)
    w["sp1ab_h"] = pm(inputs["L0_ffn_spec1a_b"], 2, F2C)
    w["sp1bb_h"] = pm(inputs["L0_ffn_spec1b_b"], 2, FC)
    w["ln1g_h"] = pm(inputs["L0_ffn_ln1_g"], 2, FC)
    w["ln1b_h"] = pm(inputs["L0_ffn_ln1_b"], 2, FC)
    w["sp2b_h"] = pm(inputs["L0_ffn_spec2_b"], 2, FC)
    w["decay"] = np.tile(f32(inputs["L0_attn_decay"]).reshape(1, 1), (1, P))
    return w


def make_in_maps(inputs, debug=False):
    w = _tile_weights(inputs)
    in_maps = []
    for k in range(8):
        xcat_fm, flags_b, gidx = _prep_core_inputs(inputs, k)
        m = dict(w)
        m["x_cat"] = xcat_fm
        m["flags"] = flags_b
        m["gidx"] = gidx
        in_maps.append(m)
    return in_maps


def kernel(**inputs):
    nc = build_nc(debug=False)
    in_maps = make_in_maps(inputs)
    res = run_bass_kernel_spmd(nc, in_maps, core_ids=list(range(8)))
    out = np.zeros((2, 16384, 1024), np.float32)
    for k in range(8):
        r = res.results[k]["out_own"]
        out[0, k * 2048:(k + 1) * 2048] = r[:2048]
        out[1, (7 - k) * 2048:(8 - k) * 2048] = r[2048:]
    return out
